# version 13
# baseline (speedup 1.0000x reference)
import numpy as np
import ml_dtypes
from contextlib import ExitStack

import concourse.bass as bass
import concourse.mybir as mybir
from concourse.bass_utils import run_bass_kernel_spmd

F32 = mybir.dt.float32
BF16 = mybir.dt.bfloat16
I8 = mybir.dt.int8
AF = mybir.ActivationFunctionType
ALU = mybir.AluOpType
AX = mybir.AxisListType
bf = ml_dtypes.bfloat16

S = 4096
DM = 2048
DI = 4096
C = 1024
DS = 16
NCT = 8
NKT = 16
NTT = 8
INTER = 8192
NOT = 64
TFF = 1024


class P:
    strict = False

    def __init__(self, nc):
        self.nc = nc
        self.ops = {e: [] for e in ("sync", "scalar", "vector", "tensor", "gpsimd")}
        self.sem_handles = {}
        self.sem_count = {}
        self.ctx = ExitStack()
        self.seq = 0

    def _app(self, engine, fn):
        import sys as _s
        f = _s._getframe(2)
        self.src = getattr(self, "src", [])
        self.src.append((self.seq, engine, f.f_lineno))
        self.ops[engine].append((self.seq, fn))
        self.seq += 1

    def sem(self, name):
        if name not in self.sem_handles:
            self.sem_handles[name] = self.ctx.enter_context(self.nc.semaphore(name))
            self.sem_count[name] = 0
        return self.sem_handles[name]

    def op(self, engine, fn, sem=None, k=1):
        if self.strict:
            self.last_tok = getattr(self, "last_tok", {})
            lt = self.last_tok.get(engine)
            if lt is not None:
                self.wait(engine, lt)
            if sem is not None:
                h = self.sem(sem)
                self.sem_count[sem] += k
                v = self.sem_count[sem]
                self._app(engine, lambda eng, fn=fn, h=h, k=k: fn(eng).then_inc(h, k))
                self.last_tok[engine] = (sem, v)
                return (sem, v)
            ssem = "strict_" + engine
            hs = self.sem(ssem)
            ks = 16 if k == 16 else 1
            self.sem_count[ssem] += ks
            v = self.sem_count[ssem]
            self._app(engine, lambda eng, fn=fn, hs=hs, ks=ks: fn(eng).then_inc(hs, ks))
            self.last_tok[engine] = (ssem, v)
            return None
        if sem is not None:
            h = self.sem(sem)
            self.sem_count[sem] += k
            v = self.sem_count[sem]
            self._app(engine, lambda eng, fn=fn, h=h, k=k: fn(eng).then_inc(h, k))
            return (sem, v)
        self._app(engine, fn)
        return None

    def dma(self, engine, out, in_, sem):
        return self.op(engine, lambda eng, o=out, i=in_: eng.dma_start(out=o, in_=i), sem, 16)

    def wait(self, engine, tok):
        if tok is None:
            return
        sem, v = tok
        h = self.sem(sem)
        assert v <= self.sem_count[sem], (sem, v, self.sem_count[sem])
        self._app(engine, lambda eng, h=h, v=v: eng.wait_ge(h, v))

    def cur(self, sem):
        return (sem, self.sem_count[sem])

    def mark(self, name):
        self.marks = getattr(self, "marks", [])
        self.marks.append((name, self.seq))


def build_program(debug=False, trunc=None, strict=False, parts="lxcpz"):
    nc = bass.Bass("TRN2", target_bir_lowering=False)
    nc.reset()
    p = P(nc)
    p.strict = strict
    ctx = p.ctx

    def inp(name, shape, dtype=F32):
        return nc.dram_tensor(name, shape, dtype, kind="ExternalInput").ap()

    x_in = inp("x", [S, DM])
    xr_in = inp("xr", [TFF, DM])
    winT = inp("winT", [NKT, 128, 2048])
    outwT = inp("outwT", [NCT, 128, DM])
    wxT = inp("wxT", [128, NCT * 33])
    alog = inp("alog", [128, 128])
    dtb = inp("dtb", [1, DI])
    convw = inp("convw", [128, NCT * 4])
    convb = inp("convb", [128, NCT])
    dcol = inp("dcol", [128, NCT])
    wg_in = inp("wg", [NOT, 128, 2048], BF16)
    wu_in = inp("wu", [NOT, 128, 2048], BF16)
    wd_in = inp("wd", [NOT, 4, 128, 512], BF16)
    sg_in = inp("sgate", [128, NOT])
    su_in = inp("sup", [128, NOT])
    sd_in = inp("sdown", [1, DM])
    constv_in = inp("constv", [128, 2])
    ident_in = inp("ident", [128, 128], BF16)
    ones_in = inp("onesb", [1, 128], BF16)
    onesf_in = inp("onesf", [1, 128], F32)

    out_ext = nc.dram_tensor("out", [TFF, DM], F32, kind="ExternalOutput").ap()
    dbg = {}
    if debug:
        for nm, shp, dt_ in (("xconv", [NCT, 128, S], BF16), ("pp", [33, S], F32),
                             ("y2", [NCT, 128, S], BF16), ("attn", [S, DM], BF16),
                             ("x2", [TFF, DM], F32), ("dtrep", [128, S], F32),
                             ("reps", [2, DS, 128, S], BF16), ("yraw", [NCT, 128, S], F32),
                             ("sb0", [128, DM], F32), ("sq0", [128, DM], F32),
                             ("colw0", [128, 16], F32), ("xln0", [128, DM], BF16),
                             ("fuse", [NOT, 128, 1024], BF16), ("x2ln", [128, 16384], BF16)):
            dbg[nm] = nc.dram_tensor("dbg_" + nm, shp, dt_, kind="ExternalOutput").ap()

    xc_hbm = nc.dram_tensor("xc_hbm", [NCT, 128, S], BF16).ap()
    z_hbm = nc.dram_tensor("z_hbm", [NCT, 128, S], BF16).ap()
    rep_hbm = nc.dram_tensor("rep_hbm", [2, DS, 128, S], BF16).ap()
    x2_hbm = nc.dram_tensor("x2_hbm", [8, 128, DM], F32).ap()
    p_dram = nc.dram_tensor("p_dram", [33, S], F32).ap()
    p2_dram = nc.dram_tensor("p2_dram", [33, S], F32).ap()
    at_dram = nc.dram_tensor("at_dram", [S, DM], BF16).ap()
    rs_dram = nc.dram_tensor("rs_dram", [TFF, DM], BF16).ap()

    arenaA_h = ctx.enter_context(nc.sbuf_tensor("arenaA", [128, 32768], BF16))
    arenaB_h = ctx.enter_context(nc.sbuf_tensor("arenaB", [128, 32768], BF16))
    ws_h = ctx.enter_context(nc.sbuf_tensor("ws", [128, 28672], BF16))
    A = arenaA_h.ap()
    B = arenaB_h.ap()
    WS = ws_h.ap()
    Af = arenaA_h.bitcast(F32).ap()
    Bf = arenaB_h.bitcast(F32).ap()
    WSf = ws_h.bitcast(F32).ap()
    WSi8 = ws_h.bitcast(I8).ap()

    ident = ctx.enter_context(nc.sbuf_tensor("identb", [128, 128], BF16)).ap()
    onesb = ctx.enter_context(nc.sbuf_tensor("onesb_t", [1, 128], BF16)).ap()
    onesf = ctx.enter_context(nc.sbuf_tensor("onesf_t", [1, 128], F32)).ap()
    nA = ctx.enter_context(nc.sbuf_tensor("nA", [128, 128], F32)).ap()
    wx_sb = ctx.enter_context(nc.sbuf_tensor("wx_sb", [128, NCT * 33], BF16)).ap()
    convw_sb = ctx.enter_context(nc.sbuf_tensor("convw_sb", [128, NCT * 4], F32)).ap()
    convb_sb = ctx.enter_context(nc.sbuf_tensor("convb_sb", [128, NCT], F32)).ap()
    dcol_sb = ctx.enter_context(nc.sbuf_tensor("dcol_sb", [128, NCT], F32)).ap()
    sg_sb = ctx.enter_context(nc.sbuf_tensor("sg_sb", [128, NOT], F32)).ap()
    su_sb = ctx.enter_context(nc.sbuf_tensor("su_sb", [128, NOT], F32)).ap()
    colw = ctx.enter_context(nc.sbuf_tensor("colw", [128, 16], F32)).ap()
    constv = ctx.enter_context(nc.sbuf_tensor("constv_t", [128, 2], F32)).ap()
    nc.const_aps.register(F32, 1e-5, constv[:, 0:1])
    nc.const_aps.register(F32, 1.0, constv[:, 1:2])

    dtrep = WSf[:, 10240:14336]
    dtb_row = Bf[0:1, 8768:12864]
    p32_row = Bf[0:1, 0:4096]
    dtv_row = Bf[0:1, 4096:8192]
    Bblk = Bf[0:16, 8192:12288]
    Cblk = Bf[0:16, 12288:16384]
    Bbf = WS[0:16, 12288:16384]
    Cbf = WS[0:16, 16384:20480]
    ps0 = ctx.enter_context(nc.psum_tensor("ps0", [128, 2048], F32)).ap()
    ps1 = ctx.enter_context(nc.psum_tensor("ps1", [128, 2048], F32)).ap()
    ps1b = ps1.bitcast(BF16)

    blk = ctx.enter_context(nc.Block())

    t = p.dma("sync", ident[:, :], ident_in, "ld")
    p.dma("sync", constv[:, :], constv_in, "ld")
    p.dma("sync", onesb[:, :], ones_in, "ld")
    p.dma("sync", onesf[:, :], onesf_in, "ld")
    p.dma("sync", nA[:, :], alog, "ld")
    p.dma("sync", convw_sb[:, :], convw, "ld")
    p.dma("sync", convb_sb[:, :], convb, "ld")
    p.dma("sync", dcol_sb[:, :], dcol, "ld")
    p.dma("sync", sg_sb[:, :], sg_in, "ld")
    p.dma("sync", su_sb[:, :], su_in, "ld")
    p.dma("sync", dtb_row[:, 0:DI], dtb, "ld")
    t_wx = p.dma("sync", WSf[:, 4096:4096 + NCT * 33], wxT, "ld")
    t_ld = p.cur("ld")

    p.wait("scalar", t_ld)
    tnexp = p.op("scalar", lambda e: e.activation(out=nA[:, :], in_=nA[:, :], func=AF.Exp), "s_s")
    p.wait("scalar", tnexp)
    p.op("scalar", lambda e: e.mul(out=nA[:, :], in_=nA[:, :], mul=-1.0), "s_s")
    t_wxcast = p.op("scalar", lambda e: e.copy(out=wx_sb[:, :], in_=WSf[:, 4096:4096 + NCT * 33]), "s_s")
    p.wait("vector", t_ld)
    tmbr = p.op("vector", lambda e: e.reduce_sum(out=colw[0:1, 8:9], in_=dtb_row[:, 0:DI], axis=AX.X), "s_v")
    p.wait("vector", tmbr)
    t_mb = p.op("vector", lambda e: e.tensor_single_scalar(out=colw[0:1, 8:9], in_=colw[0:1, 8:9], scalar=1.0 / DI, op=ALU.mult), "s_v")

    wstage = [WSf[:, 0:2048], WSf[:, 2048:4096]]
    xstage = [WSf[:, 4096:6144], WSf[:, 6144:8192]]
    xlnst = [WS[:, 16384:18432], WS[:, 18432:20480]]
    xlnT = WS[:, 20480:28672]
    convwin = [Bf[:, 0:520], Bf[:, 520:1040]]
    halo = Bf[:, 1040:1040 + NCT * 3]
    convacc = Bf[:, 1072:1584]
    sqst = Bf[:, 1600:3648]
    pT = Bf[0:33, 3648:7744]
    xcvst = [B[:, 15488:16000], B[:, 16000:16512]]
    zst = [B[:, 16512:17024], B[:, 17024:17536]]
    win_sb = A

    wcast = [None, None]
    for kt in range(NKT):
        p.wait("sync", wcast[kt % 2])
        tL = p.dma("sync", wstage[kt % 2], winT[kt], "wload")
        p.wait("scalar", tL)
        wcast[kt % 2] = p.op("scalar", lambda e, kt=kt: e.copy(
            out=win_sb[:, kt * 2048:(kt + 1) * 2048], in_=wstage[kt % 2]), "s_s")
    t_wincast = p.cur("s_s")

    p.op("vector", lambda e: e.memset(halo, 0.0), "s_v")
    t_halo = p.cur("s_v")

    x_t = x_in.rearrange("(n p) m -> n p m", p=128)
    xload_use = [t_wxcast, None]
    xln_use = [None, None]
    tp_evac = [None, None]
    mm_evac = {}
    n_ln = 0
    tv_prev = None
    t_lastpm = None
    t_xlnT_mms = None
    for tt in range(NTT):
        p.mark(f'tt{tt}')
        for i in range(4):
            ti = tt * 4 + i
            ib = n_ln % 2
            sb = xstage[ib]
            p.wait("sync", xload_use[ib])
            tx = p.dma("sync", sb, x_t[ti], "xload")
            p.wait("vector", tx)
            p.op("vector", lambda e, sb=sb: e.reduce_sum(out=colw[:, 0:1], in_=sb, axis=AX.X), None)
            p.wait("scalar", tx)
            p.wait("scalar", tv_prev)
            tsq = p.op("scalar", lambda e, sb=sb: e.activation(out=sqst, in_=sb, func=AF.Square), "s_s")
            p.wait("vector", tsq)
            tr2 = p.op("vector", lambda e: e.reduce_sum(out=colw[:, 1:2], in_=sqst, axis=AX.X), "s_v")
            tmu = p.op("vector", lambda e: e.tensor_single_scalar(out=colw[:, 2:3], in_=colw[:, 0:1], scalar=1.0 / DM, op=ALU.mult), "s_v")
            p.wait("vector", tmu)
            tm2 = p.op("vector", lambda e: e.tensor_tensor(out=colw[:, 4:5], in0=colw[:, 2:3], in1=colw[:, 2:3], op=ALU.mult), "s_v")
            p.wait("vector", tm2)
            tv = p.op("vector", lambda e: e.tensor_scalar(
                out=colw[:, 5:6], in0=colw[:, 1:2], scalar1=1.0 / DM, scalar2=colw[:, 4:5],
                op0=ALU.mult, op1=ALU.subtract), "s_v")
            tv_prev = tv
            p.wait("scalar", tv)
            tsq2 = p.op("scalar", lambda e: e.activation(out=colw[:, 6:7], in_=colw[:, 5:6], func=AF.Sqrt, bias=1e-5), "s_s")
            p.wait("vector", tsq2)
            trc = p.op("vector", lambda e: e.reciprocal(out=colw[:, 7:8], in_=colw[:, 6:7]), "s_v")
            xl = xlnst[ib]
            p.wait("vector", xln_use[ib])
            p.wait("vector", trc)
            tn = p.op("vector", lambda e, sb=sb, xl=xl: e.tensor_scalar(
                out=xl, in0=sb, scalar1=colw[:, 2:3], scalar2=colw[:, 7:8],
                op0=ALU.subtract, op1=ALU.mult), "s_v")
            xload_use[ib] = tn
            if debug and ti == 0:
                p.wait("gpsimd", tn)
                p.dma("gpsimd", dbg["sb0"], sb, "st_g")
                p.dma("gpsimd", dbg["sq0"], sqst, "st_g")
                p.dma("gpsimd", dbg["colw0"], colw[:, :], "st_g")
                tdmp = p.dma("gpsimd", dbg["xln0"], xl, "st_g")
                p.wait("scalar", tdmp)
                p.wait("vector", tdmp)
                xload_use[ib] = tdmp
            p.wait("tensor", tn)
            lastevac = None
            xlnT3 = xlnT.rearrange("p (k i) -> p k i", i=512)
            for kb in range(4):
                hb = kb % 2
                p.wait("tensor", tp_evac[hb])
                for j in range(4):
                    kt = kb * 4 + j
                    ttp = p.op("tensor", lambda e, xl=xl, kt=kt, hb=hb, j=j: e.transpose(
                        out=ps1b[:, hb * 1024 + j * 128: hb * 1024 + (j + 1) * 128],
                        in_=xl[:, kt * 128:(kt + 1) * 128],
                        identity=ident[:, :]), "s_t" if j == 3 else None)
                p.wait("vector", ttp)
                tp_evac[hb] = p.op("vector", lambda e, kb=kb, i=i, hb=hb: e.tensor_copy(
                    out=xlnT3[:, kb * 4:(kb + 1) * 4, i * 128:(i + 1) * 128],
                    in_=ps1b[:, hb * 1024: hb * 1024 + 512]), "s_v")
                lastevac = tp_evac[hb]
            xln_use[ib] = lastevac
            n_ln += 1
        t_xlnT_ready = p.cur("s_v")

        p.wait("tensor", t_xlnT_ready)
        if tt == 0:
            p.wait("tensor", t_wincast)
        for ct in (range(NCT) if "x" in parts else []):
            pb = ps0[:, (ct % 2) * 512:(ct % 2) * 512 + 512]
            p.wait("tensor", mm_evac.get(("x", ct % 2)))
            for kt in range(NKT):
                tok = p.op("tensor", lambda e, ct=ct, kt=kt, pb=pb: e.matmul(
                    out=pb, lhsT=win_sb[:, kt * 2048 + ct * 128: kt * 2048 + (ct + 1) * 128],
                    rhs=xlnT[:, kt * 512:(kt + 1) * 512],
                    start=(kt == 0), stop=(kt == NKT - 1)),
                    "s_t" if kt == NKT - 1 else None)
            t_xlnT_mms = tok
            cw = convwin[ct % 2]
            p.wait("vector", tok)
            p.wait("vector", mm_evac.get(("cw", ct % 2)))
            th2 = p.op("vector", lambda e, cw=cw, ct=ct: e.tensor_copy(out=cw[:, 0:3], in_=halo[:, ct * 3:ct * 3 + 3]), None)
            tev = p.op("vector", lambda e, cw=cw, pb=pb: e.tensor_copy(out=cw[:, 3:515], in_=pb), "s_v")
            mm_evac[("x", ct % 2)] = tev
            if "c" not in parts:
                mm_evac[("cw", ct % 2)] = tev
                continue
            p.wait("vector", mm_evac.get(("silu",)))
            p.op("vector", lambda e, cw=cw, ct=ct: e.tensor_scalar(
                out=convacc, in0=cw[:, 0:512], scalar1=convw_sb[:, ct * 4:ct * 4 + 1],
                scalar2=None, op0=ALU.mult), None)
            for j in range(1, 4):
                p.op("vector", lambda e, cw=cw, ct=ct, j=j: e.scalar_tensor_tensor(
                    out=convacc, in0=cw[:, j:j + 512], scalar=convw_sb[:, ct * 4 + j:ct * 4 + j + 1],
                    in1=convacc, op0=ALU.mult, op1=ALU.add), None)
            tg = p.op("vector", lambda e, cw=cw, ct=ct: e.tensor_copy(
                out=halo[:, ct * 3:ct * 3 + 3], in_=cw[:, 512:515]), "s_v")
            mm_evac[("cw", ct % 2)] = tg
            xcv = xcvst[ct % 2]
            p.wait("scalar", tg)
            p.wait("scalar", mm_evac.get(("xc", ct % 2)))
            ts = p.op("scalar", lambda e, ct=ct, xcv=xcv: e.activation(
                out=xcv, in_=convacc, func=AF.Silu, bias=convb_sb[:, ct:ct + 1]), "s_s")
            if "p" in parts:
                p.wait("tensor", ts)
                if ct == 0:
                    p.wait("tensor", t_lastpm)
                    p.wait("tensor", t_wxcast)
                p.op("tensor", lambda e, ct=ct, xcv=xcv: e.matmul(
                    out=ps1[0:33, 1024:1536], lhsT=wx_sb[:, ct * 33:(ct + 1) * 33],
                    rhs=xcv, start=(ct == 0), stop=(ct == NCT - 1)),
                    "s_t" if ct == NCT - 1 else None)
            p.wait("gpsimd", ts)
            tst = p.dma("gpsimd", xc_hbm[ct, :, tt * 512:(tt + 1) * 512], xcv, "st_g")
            mm_evac[("xc", ct % 2)] = tst
        if "p" in parts:
            t_pmm = p.cur("s_t")
            p.wait("vector", t_pmm)
            t_lastpm = p.op("vector", lambda e, tt=tt: e.tensor_copy(
                out=pT[:, tt * 512:(tt + 1) * 512], in_=ps1[0:33, 1024:1536]), "s_v")

        for ct in (range(NCT) if "z" in parts else []):
            pb = ps0[:, 1024 + (ct % 2) * 512: 1024 + (ct % 2) * 512 + 512]
            p.wait("tensor", mm_evac.get(("z", ct % 2)))
            for kt in range(NKT):
                tok = p.op("tensor", lambda e, ct=ct, kt=kt, pb=pb: e.matmul(
                    out=pb, lhsT=win_sb[:, kt * 2048 + 1024 + ct * 128: kt * 2048 + 1024 + (ct + 1) * 128],
                    rhs=xlnT[:, kt * 512:(kt + 1) * 512],
                    start=(kt == 0), stop=(kt == NKT - 1)),
                    "s_t" if kt == NKT - 1 else None)
            t_xlnT_mms = tok
            zt = zst[ct % 2]
            p.wait("scalar", tok)
            p.wait("scalar", mm_evac.get(("zs", ct % 2)))
            tzc = p.op("scalar", lambda e, zt=zt, pb=pb: e.copy(out=zt, in_=pb), "s_s")
            mm_evac[("z", ct % 2)] = tzc
            p.wait("gpsimd", tzc)
            tzs = p.dma("gpsimd", z_hbm[ct, :, tt * 512:(tt + 1) * 512], zt, "st_g")
            mm_evac[("zs", ct % 2)] = tzs

    p.mark('phase2')
    p.wait("gpsimd", t_lastpm)
    p.dma("gpsimd", p_dram, pT, "st_g")
    p.wait("gpsimd", p.cur("st_g"))
    t_cc = p.op("gpsimd", lambda e: e.collective_compute(
        "AllReduce", ALU.add, replica_groups=[[0, 1, 2, 3], [4, 5, 6, 7]],
        ins=[p_dram.opt()], outs=[p2_dram.opt()]), "cc")
    p.wait("gpsimd", t_cc)
    p.wait("gpsimd", t_mb)
    p.dma("gpsimd", p32_row, p2_dram[32:33, :], "ld2")
    p.dma("gpsimd", Bblk, p2_dram[0:16, :], "ld2")
    t_rows = p.dma("gpsimd", Cblk, p2_dram[16:32, :], "ld2")
    if debug:
        p.dma("gpsimd", dbg["pp"], p2_dram, "st_g")

    p.wait("scalar", t_rows)
    p.wait("scalar", t_mb)
    p.op("scalar", lambda e: e.activation(out=dtv_row, in_=p32_row, func=AF.Exp, bias=colw[0:1, 8:9]), "s_s")
    t_dt = p.op("scalar", lambda e: e.activation(out=dtv_row, in_=dtv_row, func=AF.Ln, bias=1.0), "s_s")

    p.wait("tensor", t_dt)
    p.wait("tensor", p.cur("s_v"))
    for grp in range(2):
        if grp == 1:
            p.wait("tensor", p.cur("s_v"))
        for ch4 in range(4):
            ch = grp * 4 + ch4
            tmm = p.op("tensor", lambda e, ch=ch, ch4=ch4: e.matmul(
                out=ps0[:, ch4 * 512:(ch4 + 1) * 512], lhsT=onesf[0:1, :],
                rhs=dtv_row[:, ch * 512:(ch + 1) * 512],
                start=True, stop=True), "s_t" if ch4 == 3 else None)
        p.wait("vector", tmm)
        p.op("vector", lambda e, grp=grp: e.tensor_copy(
            out=dtrep[:, grp * 2048:(grp + 1) * 2048], in_=ps0[:, :]), "s_v")
    t_dtrep = p.cur("s_v")

    p.wait("vector", t_rows)
    p.op("vector", lambda e: e.tensor_tensor(out=Bblk, in0=Bblk, in1=dtrep[0:16, :], op=ALU.mult), "s_v")
    p.op("vector", lambda e: e.tensor_copy(out=Bbf, in_=Bblk), "s_v")
    t_rowsb = p.op("vector", lambda e: e.tensor_copy(out=Cbf, in_=Cblk), "s_v")

    p.mark('reps')
    p.wait("gpsimd", t_rowsb)
    stageb = WS[0:1, 8192:12288]
    rep_store = [None, None]
    for which in range(2):
        for s in range(DS):
            par = s % 2
            src = (Bbf if which == 0 else Cbf)[s:s + 1, :]
            tstg = p.dma("gpsimd", stageb, src, "ld2")
            p.wait("tensor", tstg)
            p.wait("tensor", p.cur("s_v"))
            p.wait("tensor", p.cur("s_s"))
            lastst = None
            for grp in range(2):
                for ch4 in range(4):
                    ch = grp * 4 + ch4
                    tmm = p.op("tensor", lambda e, ch=ch, ch4=ch4, grp=grp: e.matmul(
                        out=(ps1 if grp == 0 else ps0)[:, ch4 * 512:(ch4 + 1) * 512],
                        lhsT=onesb[0:1, :], rhs=stageb[:, ch * 512:(ch + 1) * 512],
                        start=True, stop=True), "s_t" if ch4 == 3 else None)
                dst = WS[:, par * 4096 + grp * 2048: par * 4096 + (grp + 1) * 2048]
                eng = "vector" if grp == 0 else "scalar"
                p.wait(eng, tmm)
                p.wait(eng, rep_store[par])
                if eng == "vector":
                    tev = p.op("vector", lambda e, dst=dst, grp=grp: e.tensor_copy(out=dst, in_=ps1[:, :] if grp == 0 else ps0[:, :]), "s_v")
                    p.wait("gpsimd", tev)
                else:
                    tev = p.op("scalar", lambda e, dst=dst, grp=grp: e.copy(out=dst, in_=ps1[:, :] if grp == 0 else ps0[:, :]), "s_s")
                    p.wait("gpsimd", tev)
                lastst = p.dma("gpsimd", rep_hbm[which, s, :, grp * 2048:(grp + 1) * 2048], dst, "st_g")
            rep_store[par] = lastst
    t_reps = p.cur("st_g")

    p.mark('scan')
    y_acc = Af[:, 0:8192]
    xc_buf = [A[:, 16384:20480], A[:, 20480:24576]]
    z_buf = [A[:, 24576:28672], A[:, 28672:32768]]
    repB = [WS[:, 0:2048], WS[:, 2048:4096]]
    repC = [WS[:, 4096:6144], WS[:, 6144:8192]]
    d1b = [WS[:, 8192:10240], WS[:, 10240:12288]]
    hb = [WS[:, 12288:14336], WS[:, 14336:16384]]
    tmp1 = WS[:, 16384:18432]
    abar = [ps0[:, 0:2048], ps1[:, 0:2048]]

    p.wait("sync", t_reps)
    p.wait("sync", t_dtrep)
    if debug:
        p.wait("gpsimd", t_dtrep)
        p.dma("gpsimd", dbg["dtrep"], dtrep, "st_g")
        p.dma("gpsimd", dbg["reps"], rep_hbm, "st_g")
    rep_use = [None, None]
    xc_use = [None, None]
    z_use = [None, None]
    abar_use = [None, None]
    carry_tok = [None, None]
    t_scan_mm_done = p.cur("s_t")
    p.wait("scalar", p.cur("s_v"))
    p.wait("scalar", p.cur("s_s"))
    first_abar = True
    for pp in range(4):
        p.mark(f'pass{pp}')
        d0 = 2 * pp
        p.wait("sync", xc_use[0])
        txc0 = p.dma("sync", xc_buf[0], xc_hbm[d0, :, :], "scload")
        p.wait("sync", xc_use[1])
        txc1 = p.dma("sync", xc_buf[1], xc_hbm[d0 + 1, :, :], "scload")
        p.op("vector", lambda e: e.memset(y_acc, 0.0), None)
        p.wait("vector", txc1)
        for s in range(DS):
            for th in range(2):
                par = th
                p.wait("sync", rep_use[par])
                trB = p.dma("sync", repB[par], rep_hbm[0, s, :, th * 2048:(th + 1) * 2048], "scload")
                trC = p.dma("sync", repC[par], rep_hbm[1, s, :, th * 2048:(th + 1) * 2048], "scload")
                for di in range(2):
                    d = d0 + di
                    ab = abar[di]
                    if first_abar:
                        p.wait("scalar", t_scan_mm_done)
                        first_abar = False
                    p.wait("scalar", abar_use[di])
                    ta = p.op("scalar", lambda e, d=d, s=s, th=th, ab=ab: e.activation(
                        out=ab, in_=dtrep[:, th * 2048:(th + 1) * 2048], func=AF.Exp,
                        scale=nA[:, d * 16 + s: d * 16 + s + 1]), "s_s")
                    p.wait("vector", trB)
                    td1 = p.op("vector", lambda e, di=di, th=th, par=par: e.tensor_tensor(
                        out=d1b[di], in0=xc_buf[di][:, th * 2048:(th + 1) * 2048],
                        in1=repB[par], op=ALU.mult), None)
                    p.wait("vector", ta)
                    if th == 1:
                        p.wait("vector", carry_tok[di])
                    init = 0.0 if th == 0 else colw[:, 9 + di:10 + di]
                    tts = p.op("vector", lambda e, di=di, ab=ab, init=init: e.tensor_tensor_scan(
                        out=hb[di], data0=ab, data1=d1b[di], initial=init,
                        op0=ALU.mult, op1=ALU.add), "s_v")
                    abar_use[di] = tts
                    if th == 0:
                        p.wait("vector", tts)
                        carry_tok[di] = p.op("vector", lambda e, di=di: e.tensor_copy(
                            out=colw[:, 9 + di:10 + di], in_=hb[di][:, 2047:2048]), "s_v")
                    p.wait("vector", trC)
                    te1 = p.op("vector", lambda e, di=di, par=par: e.tensor_tensor(
                        out=tmp1, in0=hb[di], in1=repC[par], op=ALU.mult), "s_v")
                    rep_use[par] = te1
                    ysl = y_acc[:, di * 4096 + th * 2048: di * 4096 + (th + 1) * 2048]
                    p.op("vector", lambda e, ysl=ysl: e.tensor_tensor(
                        out=ysl, in0=ysl, in1=tmp1, op=ALU.add), None)
        p.wait("sync", z_use[0])
        tz0 = p.dma("sync", z_buf[0], z_hbm[d0, :, :], "scload")
        p.wait("sync", z_use[1])
        tz1 = p.dma("sync", z_buf[1], z_hbm[d0 + 1, :, :], "scload")
        for di in range(2):
            d = d0 + di
            ysl = y_acc[:, di * 4096:(di + 1) * 4096]
            if debug:
                p.wait("gpsimd", p.cur("s_v"))
                tydump = p.dma("gpsimd", dbg["yraw"][d], ysl, "st_g")
                p.wait("vector", tydump)
            tacc = p.op("vector", lambda e, di=di, d=d, ysl=ysl: e.scalar_tensor_tensor(
                out=ysl, in0=xc_buf[di], scalar=dcol_sb[:, d:d + 1], in1=ysl,
                op0=ALU.mult, op1=ALU.add), "s_v")
            xc_use[di] = tacc
            p.wait("scalar", tz0 if di == 0 else tz1)
            tsz = p.op("scalar", lambda e, di=di: e.activation(
                out=z_buf[di], in_=z_buf[di], func=AF.Silu), "s_s")
            p.wait("vector", tsz)
            ty2 = p.op("vector", lambda e, d=d, di=di, ysl=ysl: e.tensor_tensor(
                out=B[:, d * 4096:(d + 1) * 4096], in0=ysl, in1=z_buf[di], op=ALU.mult), "s_v")
            z_use[di] = ty2
            if debug:
                p.wait("gpsimd", ty2)
                p.dma("gpsimd", dbg["y2"][d], B[:, d * 4096:(d + 1) * 4096], "st_g")
                xc_use[di] = p.dma("gpsimd", dbg["xconv"][d], xc_buf[di], "st_g")
    t_y2 = p.cur("s_v")

    p.mark('attn')
    owcast = [None, None]
    p.wait("sync", t_y2)
    for ct in range(NCT):
        stg = WSf[:, (ct % 2) * 2048:(ct % 2) * 2048 + 2048]
        p.wait("sync", owcast[ct % 2])
        tL = p.dma("sync", stg, outwT[ct], "wload")
        p.wait("scalar", tL)
        if ct == 0:
            p.wait("scalar", t_y2)
        owcast[ct % 2] = p.op("scalar", lambda e, ct=ct, stg=stg: e.copy(
            out=A[:, ct * 2048:(ct + 1) * 2048], in_=stg), "s_s")
    t_ow = p.cur("s_s")

    at_t = at_dram.rearrange("(n p) m -> n p m", p=128)
    p.wait("tensor", t_ow)
    p.wait("tensor", t_y2)
    at_evac = [None, None]
    at_dma = [None, None]
    for tg in range(16):
        for half in range(2):
            ttl = tg * 2 + half
            psA = ps0 if half == 0 else ps1
            p.wait("tensor", at_evac[half])
            for ct in range(NCT):
                for nq in range(4):
                    tok = p.op("tensor", lambda e, ttl=ttl, ct=ct, nq=nq, psA=psA: e.matmul(
                        out=psA[:, nq * 512:(nq + 1) * 512],
                        lhsT=B[:, ct * 4096 + ttl * 128: ct * 4096 + (ttl + 1) * 128],
                        rhs=A[:, ct * 2048 + nq * 512: ct * 2048 + (nq + 1) * 512],
                        start=(ct == 0), stop=(ct == NCT - 1)),
                        "s_t" if (ct == NCT - 1 and nq == 3) else None)
            stg = WS[:, 8192 + half * 2048: 8192 + half * 2048 + 2048]
            eng = "scalar" if half == 0 else "vector"
            p.wait(eng, tok)
            p.wait(eng, at_dma[half])
            if eng == "scalar":
                te = p.op("scalar", lambda e, stg=stg, psA=psA: e.copy(out=stg, in_=psA[:, :]), "s_s")
            else:
                te = p.op("vector", lambda e, stg=stg, psA=psA: e.tensor_copy(out=stg, in_=psA[:, :]), "s_v")
            at_evac[half] = te
            p.wait("gpsimd", te)
            at_dma[half] = p.dma("gpsimd", at_t[ttl], stg, "st_g")
            if debug:
                at_dma[half] = p.dma("gpsimd", dbg["attn"].rearrange("(n p) m -> n p m", p=128)[ttl], stg, "st_g")
    p.wait("gpsimd", p.cur("st_g"))
    t_rs = p.op("gpsimd", lambda e: e.collective_compute(
        "ReduceScatter", ALU.add, replica_groups=[[0, 1, 2, 3], [4, 5, 6, 7]],
        ins=[at_dram.opt()], outs=[rs_dram.opt()]), "cc")

    p.mark('ffn')
    x2lnT = WS[:, 0:16384]
    ga = 16384
    xr_t = xr_in.rearrange("(n p) m -> n p m", p=128)
    rs_t = rs_dram.rearrange("(n p) m -> n p m", p=128)
    x2st = [WSf[:, 8192:10240], WSf[:, 10240:12288]]
    xlst = [WS[:, ga + 8192: ga + 10240], WS[:, ga + 10240: ga + 12288]]
    x2st_tok = [None] * 8
    x2_use = [None, None]
    xl_use = [None, None]
    p.wait("sync", t_rs)
    p.wait("sync", p.cur("s_v"))
    p.wait("sync", p.cur("s_s"))
    tp_evac = [None, None]
    t_x2lnT_mms = None
    tv5_prev = None
    for ti in range(8):
        ib = ti % 2
        p.wait("sync", x2_use[ib])
        tx = p.dma("sync", x2st[ib], xr_t[ti], "xload")
        p.wait("sync", xl_use[ib])
        tr = p.dma("sync", xlst[ib], rs_t[ti], "xload")
        sb = x2st[ib]
        p.wait("vector", tr)
        tadd = p.op("vector", lambda e, ib=ib: e.tensor_tensor(
            out=x2st[ib], in0=x2st[ib], in1=xlst[ib], op=ALU.add), "s_v")
        p.wait("gpsimd", tadd)
        tx2s = p.dma("gpsimd", x2_hbm[ti], sb, "st_g")
        x2st_tok[ti] = tx2s
        if debug:
            tx2s = p.dma("gpsimd", dbg["x2"].rearrange("(n p) m -> n p m", p=128)[ti], sb, "st_g")
        p.op("vector", lambda e, sb=sb: e.reduce_sum(out=colw[:, 0:1], in_=sb, axis=AX.X), None)
        p.wait("scalar", tadd)
        p.wait("scalar", tv5_prev)
        tsq = p.op("scalar", lambda e, sb=sb: e.activation(out=ps0[:, 0:2048], in_=sb, func=AF.Square), "s_s")
        p.wait("vector", tsq)
        p.op("vector", lambda e: e.reduce_sum(out=colw[:, 1:2], in_=ps0[:, 0:2048], axis=AX.X), None)
        tmu = p.op("vector", lambda e: e.tensor_single_scalar(out=colw[:, 2:3], in_=colw[:, 0:1], scalar=1.0 / DM, op=ALU.mult), "s_v")
        p.wait("vector", tmu)
        tm2 = p.op("vector", lambda e: e.tensor_tensor(out=colw[:, 4:5], in0=colw[:, 2:3], in1=colw[:, 2:3], op=ALU.mult), "s_v")
        p.wait("vector", tm2)
        tv = p.op("vector", lambda e: e.tensor_scalar(
            out=colw[:, 5:6], in0=colw[:, 1:2], scalar1=1.0 / DM, scalar2=colw[:, 4:5],
            op0=ALU.mult, op1=ALU.subtract), "s_v")
        tv5_prev = tv
        p.wait("scalar", tv)
        tsq2 = p.op("scalar", lambda e: e.activation(out=colw[:, 6:7], in_=colw[:, 5:6], func=AF.Sqrt, bias=1e-5), "s_s")
        p.wait("vector", tsq2)
        trc = p.op("vector", lambda e: e.reciprocal(out=colw[:, 7:8], in_=colw[:, 6:7]), "s_v")
        xl = xlst[ib]
        p.wait("vector", trc)
        tn = p.op("vector", lambda e, sb=sb, xl=xl: e.tensor_scalar(
            out=xl, in0=sb, scalar1=colw[:, 2:3], scalar2=colw[:, 7:8],
            op0=ALU.subtract, op1=ALU.mult), "s_v")
        x2_use[ib] = (tn[0], max(tn[1], tx2s[1]) if False else tn[1])
        x2_use[ib] = tn
        x2_store_tok = tx2s
        p.wait("tensor", tn)
        if ti == 0:
            p.wait("tensor", t_rs)
        lastevac = None
        x2lnT3 = x2lnT.rearrange("p (k i) -> p k i", i=1024)
        for kb in range(4):
            hbn = kb % 2
            p.wait("tensor", tp_evac[hbn])
            for j in range(4):
                kt = kb * 4 + j
                ttp = p.op("tensor", lambda e, xl=xl, kt=kt, hbn=hbn, j=j: e.transpose(
                    out=ps1b[:, hbn * 1024 + j * 128: hbn * 1024 + (j + 1) * 128],
                    in_=xl[:, kt * 128:(kt + 1) * 128],
                    identity=ident[:, :]), "s_t" if j == 3 else None)
            p.wait("vector", ttp)
            tp_evac[hbn] = p.op("vector", lambda e, kb=kb, ti=ti, hbn=hbn: e.tensor_copy(
                out=x2lnT3[:, kb * 4:(kb + 1) * 4, ti * 128:(ti + 1) * 128],
                in_=ps1b[:, hbn * 1024: hbn * 1024 + 512]), "s_v")
            lastevac = tp_evac[hbn]
        xl_use[ib] = lastevac
        p.wait("sync", x2_store_tok)
    t_x2lnT = p.cur("s_v")

    p.mark('gateup')
    wbf = [WS[:, ga + 2048: ga + 4096], WS[:, ga + 4096: ga + 6144]]
    gsb = WS[:, ga + 6144: ga + 7168]
    usb = WS[:, ga + 7168: ga + 8192]

    def fuse(ot):
        return (A if ot < 32 else B)[:, (ot % 32) * 1024:((ot % 32) + 1) * 1024]

    p.wait("tensor", t_x2lnT)
    p.wait("sync", t_x2lnT)
    p.wait("sync", p.cur("st_g"))
    p.wait("scalar", p.cur("st_g"))
    t_gmm = None
    t_umm = None
    t_fuse_prev = None
    for ot in range(NOT):
        p.wait("sync", t_umm)
        tcg = p.dma("sync", wbf[0][:, :], wg_in[ot], "wload")
        p.wait("tensor", tcg)
        for kt in range(NKT):
            for hh in range(2):
                tok = p.op("tensor", lambda e, kt=kt, hh=hh: e.matmul(
                    out=ps0[:, hh * 512:(hh + 1) * 512],
                    lhsT=wbf[0][:, kt * 128:(kt + 1) * 128],
                    rhs=x2lnT[:, kt * 1024 + hh * 512: kt * 1024 + (hh + 1) * 512],
                    start=(kt == 0), stop=(kt == NKT - 1)),
                    "s_t" if (kt == NKT - 1 and hh == 1) else None)
        t_gmm = tok
        p.wait("sync", t_gmm)
        tcu = p.dma("sync", wbf[1][:, :], wu_in[ot], "wload")
        p.wait("tensor", tcu)
        for kt in range(NKT):
            for hh in range(2):
                tok = p.op("tensor", lambda e, kt=kt, hh=hh: e.matmul(
                    out=ps1[:, hh * 512:(hh + 1) * 512],
                    lhsT=wbf[1][:, kt * 128:(kt + 1) * 128],
                    rhs=x2lnT[:, kt * 1024 + hh * 512: kt * 1024 + (hh + 1) * 512],
                    start=(kt == 0), stop=(kt == NKT - 1)),
                    "s_t" if (kt == NKT - 1 and hh == 1) else None)
        t_umm = tok
        p.wait("scalar", t_gmm)
        p.wait("scalar", t_fuse_prev)
        tg1 = p.op("scalar", lambda e, ot=ot: e.activation(
            out=gsb, in_=ps0[:, 0:1024], func=AF.Silu, scale=sg_sb[:, ot:ot + 1]), "s_s")
        p.wait("scalar", t_umm)
        tg2 = p.op("scalar", lambda e, ot=ot: e.activation(
            out=usb, in_=ps1[:, 0:1024], func=AF.Identity, scale=su_sb[:, ot:ot + 1]), "s_s")
        p.wait("vector", tg2)
        t_fuse_prev = p.op("vector", lambda e, ot=ot: e.tensor_tensor(
            out=fuse(ot), in0=gsb, in1=usb, op=ALU.mult), "s_v")
        if debug:
            p.wait("gpsimd", t_fuse_prev)
            p.dma("gpsimd", dbg["fuse"][ot], fuse(ot), "st_g")
    t_fuse = p.cur("s_v")
    if debug:
        p.wait("gpsimd", t_x2lnT)
        p.dma("gpsimd", dbg["x2ln"], x2lnT, "st_g")

    wdbf = [WS[:, ga + 2048: ga + 2560], WS[:, ga + 2560: ga + 3072],
            WS[:, ga + 3072: ga + 3584], WS[:, ga + 3584: ga + 4096]]
    x2q = [WSf[:, (ga + 4096) // 2: (ga + 4096) // 2 + 512],
           WSf[:, (ga + 5120) // 2: (ga + 5120) // 2 + 512]]
    t1b = [WSf[:, (ga + 6144) // 2: (ga + 6144) // 2 + 512],
           WSf[:, (ga + 7168) // 2: (ga + 7168) // 2 + 512]]
    sdrep = WS[:, ga + 8192: ga + 10240]
    out_t = out_ext.rearrange("(n p) m -> n p m", p=128)

    sdf = WSf[0:1, 10240:12288]
    sdrowb = WS[0:1, ga + 10240: ga + 12288]
    p.wait("sync", t_fuse)
    tsd = p.dma("sync", sdf, sd_in, "xload")
    p.wait("vector", tsd)
    p.wait("vector", t_fuse)
    tsdb = p.op("vector", lambda e: e.tensor_copy(out=sdrowb, in_=sdf), "s_v")
    p.wait("tensor", tsdb)
    p.wait("tensor", t_fuse)
    for ch in range(4):
        tmm = p.op("tensor", lambda e, ch=ch: e.matmul(
            out=ps0[:, ch * 512:(ch + 1) * 512], lhsT=onesb[0:1, :],
            rhs=sdrowb[:, ch * 512:(ch + 1) * 512], start=True, stop=True), "s_t")
        p.wait("vector", tmm)
        p.op("vector", lambda e, ch=ch: e.tensor_copy(
            out=sdrep[:, ch * 512:(ch + 1) * 512], in_=ps0[:, ch * 512:(ch + 1) * 512]), "s_v")
    t_sdrep = p.cur("s_v")

    p.mark('down')
    p.wait("tensor", t_sdrep)
    t1_use = [None, None]
    x2q_use = [None, None]
    t_dcast = [None, None, None, None]
    for q in range(4):
        for ot in range(NOT):
            p.wait("sync", t_dcast[ot % 4])
            twd = p.dma("sync", wdbf[ot % 4][:, :], wd_in[ot, q], "wload")
            p.wait("tensor", twd)
            for tt2 in range(8):
                psq = (ps0 if tt2 < 4 else ps1)[:, (tt2 % 4) * 512:(tt2 % 4) * 512 + 512]
                tok = p.op("tensor", lambda e, ot=ot, tt2=tt2, psq=psq: e.matmul(
                    out=psq, lhsT=fuse(ot)[:, tt2 * 128:(tt2 + 1) * 128],
                    rhs=wdbf[ot % 4][:, :],
                    start=(ot == 0), stop=(ot == NOT - 1)),
                    "s_t" if tt2 == 7 else None)
            t_dcast[ot % 4] = tok
        t_dmm = tok
        for tt2 in range(8):
            psq = (ps0 if tt2 < 4 else ps1)[:, (tt2 % 4) * 512:(tt2 % 4) * 512 + 512]
            ib = tt2 % 2
            p.wait("sync", x2q_use[ib])
            p.wait("sync", x2st_tok[tt2])
            tx2 = p.dma("sync", x2q[ib], x2_hbm[tt2, :, q * 512:(q + 1) * 512], "xload")
            p.wait("vector", t_dmm)
            p.wait("vector", t1_use[ib])
            p.op("vector", lambda e, ib=ib, q=q, psq=psq: e.tensor_tensor(
                out=t1b[ib], in0=psq, in1=sdrep[:, q * 512:(q + 1) * 512], op=ALU.mult), "s_v")
            p.wait("vector", tx2)
            to = p.op("vector", lambda e, ib=ib: e.tensor_tensor(
                out=t1b[ib], in0=t1b[ib], in1=x2q[ib], op=ALU.add), "s_v")
            x2q_use[ib] = to
            p.wait("gpsimd", to)
            t1_use[ib] = p.dma("gpsimd", out_t[tt2, :, q * 512:(q + 1) * 512], t1b[ib], "st_g")
        p.wait("tensor", p.cur("s_v"))
    p.wait("gpsimd", p.cur("st_g"))

    G = trunc if trunc is not None else p.seq + 1
    build_program.marks = getattr(p, "marks", [])
    print("program ops:", p.seq)

    @blk.sync
    def _(eng):
        for g, f in p.ops["sync"]:
            if g < G:
                f(eng)

    @blk.scalar
    def _(eng):
        for g, f in p.ops["scalar"]:
            if g < G:
                f(eng)

    @blk.vector
    def _(eng):
        for g, f in p.ops["vector"]:
            if g < G:
                f(eng)

    @blk.tensor
    def _(eng):
        for g, f in p.ops["tensor"]:
            if g < G:
                f(eng)

    @blk.gpsimd
    def _(eng):
        for g, f in p.ops["gpsimd"]:
            if g < G:
                f(eng)

    p.ctx.close()
    return nc


def make_inputs(core, kw):
    b, g = core // 4, core % 4
    W_in = np.asarray(kw["L0_attn_W_in"])
    cw = np.asarray(kw["L0_attn_conv_w"])
    cb = np.asarray(kw["L0_attn_conv_b"])
    W_x = np.asarray(kw["L0_attn_W_x"])
    A_log = np.asarray(kw["L0_attn_A_log"])
    out_W = np.asarray(kw["L0_attn_out_W"])
    Dv = np.asarray(kw["L0_attn_D"])
    sl = slice(g * C, (g + 1) * C)

    x = np.ascontiguousarray(np.asarray(kw["L0_attn_x"])[b])
    xr = np.ascontiguousarray(x[g * TFF:(g + 1) * TFF])
    winT = np.ascontiguousarray(
        np.concatenate([W_in[sl], W_in[DI + g * C: DI + (g + 1) * C]], axis=0).T
    ).reshape(NKT, 128, 2048)
    outwT = np.ascontiguousarray(out_W[:, sl].T).reshape(NCT, 128, DM)
    wxT = np.ascontiguousarray(
        W_x[:, sl].T.reshape(NCT, 128, 33).transpose(1, 0, 2).reshape(128, NCT * 33))
    alog = np.ascontiguousarray(
        A_log[sl].reshape(NCT, 128, DS).transpose(1, 0, 2).reshape(128, 128))
    convw_ = np.ascontiguousarray(
        cw[sl, 0, :].reshape(NCT, 128, 4).transpose(1, 0, 2).reshape(128, NCT * 4))
    convb_ = np.ascontiguousarray(cb[sl].reshape(NCT, 128).T)
    dcol_ = np.ascontiguousarray(Dv[sl].reshape(NCT, 128).T)
    wg = np.ascontiguousarray(
        np.asarray(kw["L0_ffn_w_gate"]).reshape(NKT, 128, NOT, 128).transpose(2, 1, 0, 3).reshape(NOT, 128, NKT * 128)).astype(bf)
    wu = np.ascontiguousarray(
        np.asarray(kw["L0_ffn_w_up"]).reshape(NKT, 128, NOT, 128).transpose(2, 1, 0, 3).reshape(NOT, 128, NKT * 128)).astype(bf)
    wd = np.ascontiguousarray(
        np.asarray(kw["L0_ffn_w_down"]).reshape(NOT, 128, 4, 512).transpose(0, 2, 1, 3)).astype(bf)
    sgate = np.ascontiguousarray(np.asarray(kw["L0_ffn_s_gate"]).reshape(NOT, 128).T)
    sup = np.ascontiguousarray(np.asarray(kw["L0_ffn_s_up"]).reshape(NOT, 128).T)
    sdown = np.ascontiguousarray(np.asarray(kw["L0_ffn_s_down"]).reshape(1, DM))
    return {
        "x": x, "xr": xr, "winT": winT, "outwT": outwT, "wxT": wxT,
        "alog": alog, "dtb": np.asarray(kw["L0_attn_dt_bias"]).reshape(1, DI).astype(np.float32),
        "convw": convw_, "convb": convb_, "dcol": dcol_,
        "wg": wg, "wu": wu, "wd": wd, "sgate": sgate, "sup": sup, "sdown": sdown,
        "constv": np.tile(np.array([[1e-5, 1.0]], np.float32), (128, 1)),
        "ident": np.eye(128, dtype=bf), "onesb": np.ones((1, 128), dtype=bf),
        "onesf": np.ones((1, 128), dtype=np.float32),
    }


_cached = {}


def kernel(**inputs):
    if "nc" not in _cached:
        _cached["nc"] = build_program()
    nc = _cached["nc"]
    in_maps = [make_inputs(c2, inputs) for c2 in range(8)]
    res = run_bass_kernel_spmd(nc, in_maps, core_ids=list(range(8)))
    out = np.empty((2, S, DM), np.float32)
    for c2 in range(8):
        b, g = c2 // 4, c2 % 4
        out[b, g * TFF:(g + 1) * TFF] = res.results[c2]["out"]
    return out
